# version 01
# baseline (speedup 1.0000x reference)
import numpy as np
import ml_dtypes
import concourse.bass as bass
import concourse.mybir as mybir
from concourse.bass_utils import run_bass_kernel_spmd

F32 = mybir.dt.float32
BF16 = mybir.dt.bfloat16
AF = mybir.ActivationFunctionType
ALU = mybir.AluOpType
BF = ml_dtypes.bfloat16

NC = 8
B = 2
S = 8192
D = 1024
DI = 2048
DIC = DI // NC
NST = 16
HMOE = 4096
E = 8
TOK = (B * S) // NC
NH, HS = 16, 64
EPS = 1e-5
EM10 = float(np.exp(np.float64(-10.0)))

ENGS = ("sync", "tensor", "vector", "scalar", "gpsimd")


class Prog:

    def __init__(self, nc, block, tick, dsems):
        self.nc = nc
        self.block = block
        self.tick = tick
        self.k = 0
        self.dsems = dsems
        self.dctr = {e: 0 for e in dsems}

    def step(self, **work):
        self.k += 1
        prev = 5 * (self.k - 1)
        for name in ENGS:
            fn = work.get(name)

            def emit(eng, fn=fn):
                eng.wait_ge(self.tick, prev)
                last = fn(eng) if fn is not None else None
                if last is None:
                    last = eng.nop()
                last.then_inc(self.tick, 1)
            getattr(self.block, name)(emit)

    def dmas(self, eng_name):
        sem = self.dsems[eng_name]

        def go(eng, transfers, **kw):
            for out, in_ in transfers:
                eng.dma_start(out=out, in_=in_, **kw).then_inc(sem, 16)
                self.dctr[eng_name] += 16
            eng.wait_ge(sem, self.dctr[eng_name])
            return eng.nop()
        return go


import contextlib


def alloc(nc, st, sbufs=(), psums=(), sems=()):
    d = {}
    for name, shape, dt in sbufs:
        d[name] = st.enter_context(nc.sbuf_tensor(name, list(shape), dt))
    for name, shape, dt in psums:
        d[name] = st.enter_context(nc.psum_tensor(name, list(shape), dt))
    for name in sems:
        d[name] = st.enter_context(nc.semaphore(name))
    return d



def _col8(v):
    return np.ascontiguousarray(v.reshape(8, 128).T)


def host_prep(I):
    P = {}
    x = I["L0_attn_x"]
    xf = np.asarray(x).reshape(B * S, D)
    P["x_fm"] = [np.ascontiguousarray(xf[c * TOK:(c + 1) * TOK].T) for c in range(NC)]
    P["lng"] = _col8(I["L0_attn_ln_g"])
    P["lnb"] = _col8(I["L0_attn_ln_b"])
    in_w = I["L0_attn_in_w"]
    P["inwT"] = []
    for c in range(NC):
        rows = np.concatenate([in_w[DIC * c:DIC * (c + 1)],
                               in_w[DI + DIC * c:DI + DIC * (c + 1)]], axis=0)
        P["inwT"].append(np.ascontiguousarray(rows.T))
    cw_ = I["L0_attn_conv_w"][:, 0, :]
    P["convw"] = [np.ascontiguousarray(
        cw_[DIC * c:DIC * (c + 1)].reshape(2, 128, 4).transpose(1, 0, 2).reshape(128, 8))
        for c in range(NC)]
    P["convb"] = [np.ascontiguousarray(
        I["L0_attn_conv_b"][DIC * c:DIC * (c + 1)].reshape(2, 128).T) for c in range(NC)]
    xp_w = I["L0_attn_xproj_w"]
    P["xprojT"] = [np.ascontiguousarray(
        xp_w[:, DIC * c:DIC * (c + 1)].T.reshape(2, 128, 33).transpose(1, 0, 2).reshape(128, 66))
        for c in range(NC)]
    P["dtw"] = [np.ascontiguousarray(I["L0_attn_dt_w"][DIC * c:DIC * (c + 1), 0]
                                     .reshape(2, 128).T) for c in range(NC)]
    P["dtb"] = [np.ascontiguousarray(I["L0_attn_dt_b"][DIC * c:DIC * (c + 1)]
                                     .reshape(2, 128).T) for c in range(NC)]
    P["Dc"] = [np.ascontiguousarray(I["L0_attn_D"][DIC * c:DIC * (c + 1)]
                                    .reshape(2, 128).T) for c in range(NC)]
    P["A_log"] = np.ascontiguousarray(I["L0_attn_A_log"].reshape(1, 16))
    P["outwT"] = np.ascontiguousarray(I["L0_attn_out_w"].T)
    P["gate_w"] = np.ascontiguousarray(I["L0_ffn_gate_w"])
    P["w1"] = I["L0_ffn_w1"]
    P["b1c"] = np.ascontiguousarray(
        I["L0_ffn_b1"].reshape(8, 32, 128).transpose(2, 0, 1).reshape(128, 8 * 32))
    P["w2"] = I["L0_ffn_w2"]
    P["b2row"] = np.ascontiguousarray(I["L0_ffn_b2"].reshape(1, 8 * 1024))
    P["gamma"] = _col8(I["L0_ffn_gamma"])
    P["beta"] = _col8(I["L0_ffn_beta"])
    P["wattnT"] = np.ascontiguousarray(I["L1_attn_w_attn"].T)
    P["wprojT"] = np.ascontiguousarray(I["L1_attn_w_proj"].T)
    P["W_base"] = np.ascontiguousarray(I["L1_ffn_W_base"])
    P["bbase"] = _col8(I["L1_ffn_b_base"])
    P["Wd1"] = I["L1_ffn_Wd1"]
    P["bd1c"] = np.ascontiguousarray(
        I["L1_ffn_bd1"].reshape(8, 2, 128).transpose(2, 0, 1).reshape(128, 16))
    P["Wd2"] = I["L1_ffn_Wd2"]
    P["bd2c"] = np.ascontiguousarray(
        I["L1_ffn_bd2"].reshape(8, 8, 128).transpose(2, 0, 1).reshape(128, 64))
    P["Wad"] = I["L1_ffn_Wa_down"]
    P["badc"] = np.ascontiguousarray(I["L1_ffn_ba_down"].T)
    P["Wau"] = I["L1_ffn_Wa_up"]
    P["baup8"] = np.ascontiguousarray(I["L1_ffn_ba_up"])
    selm2 = np.zeros((8, 2), np.float32)
    selm2[0:4, 0] = 1.0
    selm2[4:8, 1] = 1.0
    P["selb"] = selm2
    selm = np.zeros((16, 16 * 128), np.float32)
    for n in range(16):
        selm[n, n * 128:(n + 1) * 128] = 1.0
    P["sel"] = selm
    m = np.zeros((4, 128, 512), np.float32)
    for r in range(4):
        for p in range(128):
            m[r, p, :] = (np.arange(512) >= p + 128 * r)
    P["dmask"] = np.ascontiguousarray(m.reshape(4 * 128, 512))
    return P



def emit_ln_stats(prog, x_tile, sq_sb, ps_mu, ps_sq, ones_col,
                  mvrow, arow, brow, eps_col, ncols=512, n_chunks=4, nd=8,
                  nfeat=1024):
    inv = 1.0 / nfeat
    G = sq_sb.shape[1]
    for n in range(n_chunks):
        n_sl = slice(n * ncols, (n + 1) * ncols)
        for g0 in range(0, nd, G):
            def sqs(s, g0=g0, n_sl=n_sl):
                last = None
                for i in range(G):
                    last = s.activation(sq_sb[:, i, :], x_tile(g0 + i, n_sl), AF.Square)
                return last
            prog.step(scalar=sqs)

            def mms(t, g0=g0, n_sl=n_sl):
                last = None
                for i in range(G):
                    j = g0 + i
                    t.matmul(ps_mu[0:1, :], ones_col[:, 0:1], x_tile(j, n_sl),
                             start=(j == 0), stop=(j == nd - 1))
                    last = t.matmul(ps_sq[0:1, :], ones_col[:, 0:1], sq_sb[:, i, :],
                                    start=(j == 0), stop=(j == nd - 1))
                return last
            prog.step(tensor=mms)

        mu = mvrow[0:1, 0:ncols]
        va = mvrow[0:1, ncols:2 * ncols]

        def stats(v, n_sl=n_sl, mu=mu, va=va):
            v.tensor_scalar(mu, ps_mu[0:1, :], inv, None, ALU.mult)
            v.tensor_scalar(va, ps_sq[0:1, :], inv, None, ALU.mult)
            v.tensor_tensor(arow[0:1, n_sl], mu, mu, ALU.mult)
            return v.tensor_tensor(va, va, arow[0:1, n_sl], ALU.subtract)
        prog.step(vector=stats)

        def rstd(s, n_sl=n_sl, va=va):
            s.activation(arow[0:1, n_sl], va, AF.Ln, bias=eps_col[0:1, 0:1])
            return s.activation(arow[0:1, n_sl], arow[0:1, n_sl], AF.Exp, scale=-0.5)
        prog.step(scalar=rstd)

        def bro(v, n_sl=n_sl, mu=mu):
            return v.scalar_tensor_tensor(brow[0:1, n_sl], mu, -1.0,
                                          arow[0:1, n_sl], ALU.mult, ALU.mult)
        prog.step(vector=bro)


def emit_ln_apply(prog, x_tile, out_fns, ps_a, ps_b, ones_row, arow, brow, tmp_sb,
                  ncols=512, n_chunks=4, nd=8):
    for n in range(n_chunks):
        n_sl = slice(n * ncols, (n + 1) * ncols)

        def bcast(t, n_sl=n_sl):
            t.matmul(ps_a[:, :], ones_row[0:1, :], arow[0:1, n_sl], start=True, stop=True)
            return t.matmul(ps_b[:, :], ones_row[0:1, :], brow[0:1, n_sl],
                            start=True, stop=True)
        prog.step(tensor=bcast)

        def apply(v, n=n, n_sl=n_sl):
            last = None
            for j in range(nd):
                v.tensor_tensor(tmp_sb[:, 0:ncols], x_tile(j, n_sl), ps_a[:, :], ALU.mult)
                v.tensor_tensor(tmp_sb[:, 0:ncols], tmp_sb[:, 0:ncols], ps_b[:, :], ALU.add)
                last = out_fns(j, n, n_sl, v, tmp_sb[:, 0:ncols])
            return last
        prog.step(vector=apply)


def emit_ln_fused(prog, x_tile, out_fn, sq_sb, ps_mu, ps_sq, ps_a, ps_b,
                  ones_col, mv4, eps_col, ncols=512, n_chunks=4, nd=8, nfeat=1024,
                  post_chunk=None):
    inv = 1.0 / nfeat
    G = sq_sb.shape[1]
    for n in range(n_chunks):
        n_sl = slice(n * ncols, (n + 1) * ncols)
        for g0 in range(0, nd, G):
            def sqs(s, g0=g0, n_sl=n_sl):
                last = None
                for i in range(G):
                    last = s.activation(sq_sb[:, i, :], x_tile(g0 + i, n_sl), AF.Square)
                return last
            prog.step(scalar=sqs)

            def mms(t, g0=g0, n_sl=n_sl):
                last = None
                for i in range(G):
                    j = g0 + i
                    t.matmul(ps_mu[0:1, :], ones_col[:, 0:1], x_tile(j, n_sl),
                             start=(j == 0), stop=(j == nd - 1))
                    last = t.matmul(ps_sq[0:1, :], ones_col[:, 0:1], sq_sb[:, i, :],
                                    start=(j == 0), stop=(j == nd - 1))
                return last
            prog.step(tensor=mms)

        mu = mv4[0:1, 0:ncols]
        va = mv4[0:1, ncols:2 * ncols]
        ar = mv4[0:1, 2 * ncols:3 * ncols]
        br = mv4[0:1, 3 * ncols:4 * ncols]

        def stats(v, mu=mu, va=va, ar=ar):
            v.tensor_scalar(mu, ps_mu[0:1, :], inv, None, ALU.mult)
            v.tensor_scalar(va, ps_sq[0:1, :], inv, None, ALU.mult)
            v.tensor_tensor(ar, mu, mu, ALU.mult)
            return v.tensor_tensor(va, va, ar, ALU.subtract)
        prog.step(vector=stats)

        def rstd(s, va=va, ar=ar):
            s.activation(ar, va, AF.Ln, bias=eps_col[0:1, 0:1])
            return s.activation(ar, ar, AF.Exp, scale=-0.5)
        prog.step(scalar=rstd)

        def bro(v, mu=mu, ar=ar, br=br):
            return v.scalar_tensor_tensor(br, mu, -1.0, ar, ALU.mult, ALU.mult)
        prog.step(vector=bro)

        def bcast(t, ar=ar, br=br):
            t.matmul(ps_a[:], ones_col[0:1, 0:128], ar, start=True, stop=True)
            return t.matmul(ps_b[:], ones_col[0:1, 0:128], br, start=True, stop=True)
        prog.step(tensor=bcast)

        def apply(v, n=n, n_sl=n_sl):
            last = None
            for j in range(nd):
                last = out_fn(j, n, n_sl, v, x_tile(j, n_sl), ps_a, ps_b)
            return last
        prog.step(vector=apply)
        if post_chunk is not None:
            post_chunk(prog, n, n_sl)



def build_L1(nc):
    x = nc.dram_tensor("x_fm", [D, TOK], F32, kind="ExternalInput").ap()
    lng = nc.dram_tensor("lng", [128, 8], F32, kind="ExternalInput").ap()
    lnb = nc.dram_tensor("lnb", [128, 8], F32, kind="ExternalInput").ap()
    xn0 = nc.dram_tensor("xn0", [D, TOK], BF16, kind="ExternalOutput").ap()
    with (
        nc.sbuf_tensor("xT", [128, 8, TOK], F32) as xT,
        nc.sbuf_tensor("xn", [128, 8, TOK], BF16) as xn,
        nc.sbuf_tensor("sq", [128, 8, 512], F32) as sq,
        nc.sbuf_tensor("rows", [1, 4 * TOK], F32) as rows,
        nc.sbuf_tensor("gb", [128, 16], F32) as gb,
        nc.sbuf_tensor("ones", [128, 128], F32) as ones,
        nc.sbuf_tensor("consts", [128, 4], F32) as consts,
        nc.sbuf_tensor("tmp", [128, 512], F32) as tmp,
        nc.psum_tensor("ps0", [128, 512], F32) as ps0,
        nc.psum_tensor("ps1", [128, 512], F32) as ps1,
        nc.psum_tensor("ps2", [128, 512], F32) as ps2,
        nc.psum_tensor("ps3", [128, 512], F32) as ps3,
        nc.semaphore("tick") as tick,
        nc.semaphore("dm_sync") as dm_sync,
        nc.semaphore("dm_gp") as dm_gp,
        nc.Block() as block,
    ):
        prog = Prog(nc, block, tick, {"sync": dm_sync, "gpsimd": dm_gp})
        dma_s = prog.dmas("sync")
        arow = rows[:, 2 * TOK:3 * TOK]
        brow = rows[:, 3 * TOK:4 * TOK]

        prog.step(sync=lambda e: dma_s(e, [
            (xT[:], x.rearrange("(j p) t -> p j t", p=128)),
            (gb[:, 0:8], lng), (gb[:, 8:16], lnb),
        ]), vector=lambda v: (v.memset(ones[:], 1.0),
                              v.memset(consts[:, 0:1], EPS),
                              v.memset(consts[:, 1:2], -10.0))[-1])

        emit_ln_stats(prog, lambda j, n_sl: xT[:, j, n_sl], sq, ps0, ps1,
                      ones, rows[:, 0:1024], arow, brow, consts)

        def out_fn(j, n, n_sl, v, normed):
            return v.tensor_scalar(xn[:, j, n_sl], normed, gb[:, j:j + 1],
                                   gb[:, 8 + j:9 + j], ALU.mult, ALU.add)
        emit_ln_apply(prog, lambda j, n_sl: xT[:, j, n_sl], out_fn,
                      ps2, ps3, ones, arow, brow, tmp)

        prog.step(sync=lambda e: dma_s(e, [
            (xn0.rearrange("(j p) t -> p j t", p=128), xn[:]),
        ]))
    return nc



def build_L2(nc):
    xn0 = nc.dram_tensor("xn0_all", [D, B * S], BF16, kind="ExternalInput").ap()
    inwT = nc.dram_tensor("inwT", [D, 4 * 128], F32, kind="ExternalInput").ap()
    convw = nc.dram_tensor("convw", [128, 8], F32, kind="ExternalInput").ap()
    convb = nc.dram_tensor("convb", [128, 2], F32, kind="ExternalInput").ap()
    xprojT = nc.dram_tensor("xprojT", [128, 66], F32, kind="ExternalInput").ap()
    xs_out = nc.dram_tensor("xs_out", [2 * DIC, S], F32, kind="ExternalOutput").ap()
    z_out = nc.dram_tensor("z_out", [2 * DIC, S], BF16, kind="ExternalOutput").ap()
    xp_out = nc.dram_tensor("xp_out", [66, S], F32, kind="ExternalOutput").ap()
    TCH = 2048
    st = contextlib.ExitStack()
    t_ = alloc(nc, st, sbufs=[
        ("inw", (128, 8, 4 * 128), BF16), ("xnc", (128, 8, TCH), BF16),
        ("xi", (128, 2, TCH + 3), F32), ("xc", (128, 2, TCH), F32),
        ("xsc", (128, 2, TCH), F32), ("zst", (128, 2, TCH), BF16),
        ("cw", (128, 8), F32), ("cb", (128, 2), F32),
        ("xpj", (128, 2, 33), F32), ("xpst", (33, 2, TCH), F32),
        ("carry", (128, 2, 3), F32),
    ], psums=[("ps0", (128, 512), F32), ("ps1", (128, 512), F32),
              ("ps2", (128, 512), F32), ("ps3", (128, 512), F32)],
       sems=["tick", "dm_sync", "dm_gp"])
    inw, xnc, xi, xc, xsc, zst = t_["inw"], t_["xnc"], t_["xi"], t_["xc"], t_["xsc"], t_["zst"]
    cw_sb, cb_sb, xpj, xpst, carry = t_["cw"], t_["cb"], t_["xpj"], t_["xpst"], t_["carry"]
    ps0, ps1, ps2, ps3 = t_["ps0"], t_["ps1"], t_["ps2"], t_["ps3"]
    tick, dm_sync, dm_gp = t_["tick"], t_["dm_sync"], t_["dm_gp"]
    with nc.Block() as block:
        prog = Prog(nc, block, tick, {"sync": dm_sync, "gpsimd": dm_gp})
        dma_s = prog.dmas("sync")
        dma_g = prog.dmas("gpsimd")
        ps = [ps0, ps1, ps2, ps3]

        prog.step(gpsimd=lambda g: dma_g(g, [
            (inw[:], inwT.rearrange("(k p) m -> p k m", p=128)),
        ]), sync=lambda e: dma_s(e, [
            (cw_sb[:], convw), (cb_sb[:], convb),
            (xpj[:], xprojT.rearrange("p (d m) -> p d m", d=2)),
        ]))

        for b in range(B):
            for ch in range(S // TCH):
                t0 = b * S + ch * TCH

                def ldxn(e, t0=t0):
                    return dma_s(e, [(xnc[:], xn0[:, t0:t0 + TCH]
                                      .rearrange("(j p) t -> p j t", p=128))])
                prog.step(sync=ldxn)

                for n in range(TCH // 512):
                    n_sl = slice(n * 512, (n + 1) * 512)

                    def mms(t, n_sl=n_sl):
                        last = None
                        for m in range(4):
                            for k in range(8):
                                last = t.matmul(ps[m][:], inw[:, k, m * 128:(m + 1) * 128],
                                                xnc[:, k, n_sl],
                                                start=(k == 0), stop=(k == 7))
                        return last

                    def evac(v, n=n, n_sl=n_sl, ch=ch):
                        x_sl = slice(3 + n * 512, 3 + (n + 1) * 512)
                        if n == 0 and True:
                            if ch == 0:
                                v.memset(carry[:], 0.0)
                            v.tensor_copy(xi[:, :, 0:3], carry[:])
                        v.tensor_copy(xi[:, 0, x_sl], ps[0][:])
                        v.tensor_copy(xi[:, 1, x_sl], ps[1][:])
                        v.tensor_copy(zst[:, 0, n_sl], ps[2][:])
                        return v.tensor_copy(zst[:, 1, n_sl], ps[3][:])
                    prog.step(tensor=mms)
                    prog.step(vector=evac)

                def conv(v):
                    last = None
                    for dt in range(2):
                        v.tensor_scalar(xc[:, dt, :], xi[:, dt, 0:TCH],
                                        cw_sb[:, dt * 4:dt * 4 + 1], None, ALU.mult)
                        for k in range(1, 4):
                            last = v.scalar_tensor_tensor(
                                xc[:, dt, :], xi[:, dt, k:k + TCH],
                                cw_sb[:, dt * 4 + k:dt * 4 + k + 1],
                                xc[:, dt, :], ALU.mult, ALU.add)
                    return last
                prog.step(vector=conv)

                def silu_save(s):
                    s.activation(xsc[:, 0, :], xc[:, 0, :], AF.Silu, bias=cb_sb[:, 0:1])
                    return s.activation(xsc[:, 1, :], xc[:, 1, :], AF.Silu, bias=cb_sb[:, 1:2])

                def carry_save(v):
                    return v.tensor_copy(carry[:], xi[:, :, TCH:TCH + 3])
                prog.step(scalar=silu_save, vector=carry_save)

                def xpmm(t):
                    last = None
                    for n in range(4):
                        for dt in range(2):
                            last = t.matmul(ps[n][0:33, :], xpj[:, dt, :],
                                            xsc[:, dt, n * 512:(n + 1) * 512],
                                            start=(dt == 0), stop=(dt == 1))
                    return last
                prog.step(tensor=xpmm)

                def xpev(v):
                    last = None
                    for n in range(4):
                        last = v.tensor_copy(xpst[:, 0, n * 512:(n + 1) * 512], ps[n][0:33, :])
                    return last
                prog.step(vector=xpev)

                def store(e, b=b, ch=ch):
                    c_sl = slice(ch * TCH, (ch + 1) * TCH)
                    return dma_s(e, [
                        (xs_out[b * 256:b * 256 + 128, c_sl], xsc[:, 0, :]),
                        (xs_out[b * 256 + 128:b * 256 + 256, c_sl], xsc[:, 1, :]),
                        (z_out[b * 256:b * 256 + 128, c_sl], zst[:, 0, :]),
                        (z_out[b * 256 + 128:b * 256 + 256, c_sl], zst[:, 1, :]),
                        (xp_out[b * 33:b * 33 + 33, c_sl], xpst[:, 0, :]),
                    ])
                prog.step(sync=store)
    st.close()
    return nc



def build_L3(nc):
    xs_in = nc.dram_tensor("xs_in", [2 * DIC, S], F32, kind="ExternalInput").ap()
    z_in = nc.dram_tensor("z_in", [2 * DIC, S], BF16, kind="ExternalInput").ap()
    xpp = nc.dram_tensor("xp_parts", [NC, 66, S], F32, kind="ExternalInput").ap()
    alog = nc.dram_tensor("A_log", [1, 16], F32, kind="ExternalInput").ap()
    dtw = nc.dram_tensor("dtw", [128, 2], F32, kind="ExternalInput").ap()
    dtb = nc.dram_tensor("dtb", [128, 2], F32, kind="ExternalInput").ap()
    Dc = nc.dram_tensor("Dc", [128, 2], F32, kind="ExternalInput").ap()
    sel = nc.dram_tensor("sel", [16, 16 * 128], F32, kind="ExternalInput").ap()
    y_out = nc.dram_tensor("y_out", [2 * DIC, S], BF16, kind="ExternalOutput").ap()
    xp_dram = nc.dram_tensor("xp_scr", [66, S], F32, kind="Internal").ap()
    TCH = 2048
    st = contextlib.ExitStack()
    t_ = alloc(nc, st, sbufs=[
        ("xp", (66, S), F32), ("xpb", (66, TCH), F32), ("xp0", (1, TCH), F32),
        ("B_t", (16, TCH), F32), ("C_t", (16, TCH), F32),
        ("sel_sb", (16, 16 * 128), F32),
        ("Arep", (128, 16), F32), ("Arow", (1, 16), F32), ("cols", (128, 8), F32),
        ("ones", (128, 128), F32), ("hstate", (128, 64), F32),
        ("xs_t", (128, TCH), F32), ("z_t", (128, TCH), BF16),
        ("delta", (128, TCH), F32), ("u_t", (128, TCH), F32),
        ("dA", (128, TCH), F32), ("dBx", (128, TCH), F32),
        ("h_t", (128, TCH), F32), ("yacc", (128, TCH), F32),
        ("sz", (128, TCH), F32), ("ybf", (128, TCH), BF16),
    ], psums=[(f"ps{i}", (128, 512), F32) for i in range(8)],
       sems=["tick", "dm_sync", "dm_gp"])
    (xp_sb, xpb, xp0, B_t, C_t, sel_sb, Arep, Arow, cols, ones, hstate, xs_t, z_t,
     delta, u_t, dA, dBx, h_t, yacc, sz, ybf) = (
        t_["xp"], t_["xpb"], t_["xp0"], t_["B_t"], t_["C_t"], t_["sel_sb"], t_["Arep"],
        t_["Arow"], t_["cols"], t_["ones"], t_["hstate"], t_["xs_t"], t_["z_t"],
        t_["delta"], t_["u_t"], t_["dA"], t_["dBx"], t_["h_t"], t_["yacc"],
        t_["sz"], t_["ybf"])
    ps0, ps1, ps2, ps3, ps4, ps5, ps6, ps7 = [t_[f"ps{i}"] for i in range(8)]
    tick, dm_sync, dm_gp = t_["tick"], t_["dm_sync"], t_["dm_gp"]
    with nc.Block() as block:
        prog = Prog(nc, block, tick, {"sync": dm_sync, "gpsimd": dm_gp})
        dma_s = prog.dmas("sync")
        psB = [ps0, ps1, ps2, ps3]
        psC = [ps4, ps5, ps6, ps7]

        prog.step(sync=lambda e: dma_s(e, [
            (Arow[:], alog), (cols[:, 0:2], dtw), (cols[:, 2:4], dtb),
            (cols[:, 4:6], Dc), (sel_sb[:], sel),
        ]), vector=lambda v: (v.memset(ones[:], 1.0), v.memset(hstate[:], 0.0))[-1])

        prog.step(scalar=lambda s: s.activation(Arow[:], Arow[:], AF.Exp))
        prog.step(vector=lambda v: v.tensor_scalar(Arow[:], Arow[:], -1.0, None, ALU.mult))
        prog.step(tensor=lambda t: t.matmul(ps0[:, 0:16], ones[0:1, 0:128], Arow[:],
                                            start=True, stop=True))
        prog.step(vector=lambda v: v.tensor_copy(Arep[:], ps0[:, 0:16]))

        for ch in range(S // TCH):
            c_sl = slice(ch * TCH, (ch + 1) * TCH)
            prog.step(sync=lambda e, c_sl=c_sl: dma_s(e, [(xp_sb[:, c_sl], xpp[0, :, c_sl])]))
            for c in range(1, NC):
                prog.step(sync=lambda e, c=c, c_sl=c_sl: dma_s(
                    e, [(xpb[:], xpp[c, :, c_sl])]))
                prog.step(vector=lambda v, c_sl=c_sl: v.tensor_tensor(
                    xp_sb[:, c_sl], xp_sb[:, c_sl], xpb[:], ALU.add))
        prog.step(sync=lambda e: dma_s(e, [(xp_dram[:, :], xp_sb[:, :])]))

        for b in range(B):
            for g in range(2):
                for ts in range(S // TCH):
                    t_sl = slice(ts * TCH, (ts + 1) * TCH)
                    row0 = b * 256 + g * 128

                    def lds(e, b=b, row0=row0, t_sl=t_sl):
                        return dma_s(e, [
                            (xs_t[:], xs_in[row0:row0 + 128, t_sl]),
                            (z_t[:], z_in[row0:row0 + 128, t_sl]),
                            (xp0[:], xp_dram[33 * b:33 * b + 1, t_sl]),
                            (B_t[:], xp_dram[1 + 33 * b:17 + 33 * b, t_sl]),
                            (C_t[:], xp_dram[17 + 33 * b:33 + 33 * b, t_sl]),
                        ])
                    prog.step(sync=lds)
                    def x0b(t, t_sl=t_sl):
                        last = None
                        for n in range(4):
                            last = t.matmul(psB[n][:], ones[0:1, 0:128],
                                            xp0[0:1, n * 512:(n + 1) * 512],
                                            start=True, stop=True)
                        return last
                    prog.step(tensor=x0b)

                    def sdelta(s, g=g):
                        last = None
                        for n in range(4):
                            last = s.activation(delta[:, n * 512:(n + 1) * 512], psB[n][:],
                                                AF.Exp, scale=cols[:, g:g + 1],
                                                bias=cols[:, 2 + g:3 + g])
                        return last
                    prog.step(scalar=sdelta)
                    prog.step(scalar=lambda s: s.activation(delta[:], delta[:], AF.Ln, bias=1.0))
                    prog.step(vector=lambda v: v.tensor_tensor(u_t[:], delta[:], xs_t[:], ALU.mult))

                    for n in range(NST):
                        hcol = (b * 2 + g) * 16 + n

                        def brep(t, n=n):
                            last = None
                            for q in range(4):
                                q_sl = slice(q * 512, (q + 1) * 512)
                                t.matmul(psB[q][:], sel_sb[:, n * 128:(n + 1) * 128],
                                         B_t[:, q_sl], start=True, stop=True)
                                last = t.matmul(psC[q][:], sel_sb[:, n * 128:(n + 1) * 128],
                                                C_t[:, q_sl], start=True, stop=True)
                            return last

                        def dAexp(s, n=n):
                            return s.activation(dA[:], delta[:], AF.Exp,
                                                scale=Arep[:, n:n + 1])
                        prog.step(tensor=brep, scalar=dAexp)

                        def scan_grp(v, n=n, hcol=hcol):
                            for q in range(4):
                                v.tensor_tensor(dBx[:, q * 512:(q + 1) * 512],
                                                u_t[:, q * 512:(q + 1) * 512],
                                                psB[q][:], ALU.mult)
                            v.tensor_tensor_scan(h_t[:], dA[:], dBx[:],
                                                 hstate[:, hcol:hcol + 1], ALU.mult, ALU.add)
                            v.tensor_copy(hstate[:, hcol:hcol + 1], h_t[:, TCH - 1:TCH])
                            last = None
                            for q in range(4):
                                q_sl = slice(q * 512, (q + 1) * 512)
                                if n == 0:
                                    last = v.tensor_tensor(yacc[:, q_sl], h_t[:, q_sl],
                                                           psC[q][:], ALU.mult)
                                else:
                                    v.tensor_tensor(dBx[:, q_sl], h_t[:, q_sl],
                                                    psC[q][:], ALU.mult)
                                    last = v.tensor_tensor(yacc[:, q_sl], yacc[:, q_sl],
                                                           dBx[:, q_sl], ALU.add)
                            return last
                        prog.step(vector=scan_grp)

                    def ypre(v, g=g):
                        return v.scalar_tensor_tensor(yacc[:], xs_t[:], cols[:, 4 + g:5 + g],
                                                      yacc[:], ALU.mult, ALU.add)
                    prog.step(vector=ypre)
                    prog.step(scalar=lambda s: s.activation(sz[:], z_t[:], AF.Silu))
                    prog.step(vector=lambda v: v.tensor_tensor(ybf[:], yacc[:], sz[:], ALU.mult))
                    prog.step(sync=lambda e, row0=row0, t_sl=t_sl: dma_s(e, [
                        (y_out[row0:row0 + 128, t_sl], ybf[:]),
                    ]))
    st.close()
    return nc



def build_L4(nc, dbg=False):
    y_in = nc.dram_tensor("y_cols", [DI, TOK], BF16, kind="ExternalInput").ap()
    x_in = nc.dram_tensor("x_fm", [D, TOK], F32, kind="ExternalInput").ap()
    outwT = nc.dram_tensor("outwT", [DI, D], F32, kind="ExternalInput").ap()
    gamma = nc.dram_tensor("gamma", [128, 8], F32, kind="ExternalInput").ap()
    beta = nc.dram_tensor("beta", [128, 8], F32, kind="ExternalInput").ap()
    gate_w = nc.dram_tensor("gate_w", [D, 8], F32, kind="ExternalInput").ap()
    w1 = nc.dram_tensor("w1", [E, D, HMOE], F32, kind="ExternalInput").ap()
    b1c = nc.dram_tensor("b1c", [128, E * 32], F32, kind="ExternalInput").ap()
    w2 = nc.dram_tensor("w2", [E, HMOE, D], F32, kind="ExternalInput").ap()
    b2row = nc.dram_tensor("b2row", [1, E * D], F32, kind="ExternalInput").ap()
    wattnT = nc.dram_tensor("wattnT", [D, 3 * D], F32, kind="ExternalInput").ap()
    ident = nc.dram_tensor("ident", [128, 128], F32, kind="ExternalInput").ap()
    x2_out = nc.dram_tensor("x2_out", [D, TOK], F32, kind="ExternalOutput").ap()
    if dbg:
        dbg_s = nc.dram_tensor("dbg_s", [D, TOK], F32, kind="ExternalOutput").ap()
        dbg_rows = nc.dram_tensor("dbg_rows", [1, 4 * TOK], F32, kind="ExternalOutput").ap()
        dbg_xn1 = nc.dram_tensor("dbg_xn1", [D, TOK], BF16, kind="ExternalOutput").ap()
        dbg_cw = nc.dram_tensor("dbg_cw", [128, 16 * 8], F32, kind="ExternalOutput").ap()
        dbg_acc = nc.dram_tensor("dbg_acc", [128, 16 * 1024], F32, kind="ExternalOutput").ap()
        dbg_acc2 = nc.dram_tensor("dbg_acc2", [128, 16 * 1024], F32, kind="ExternalOutput").ap()
        dbg_g2 = nc.dram_tensor("dbg_g2", [128, 16 * 8], F32, kind="ExternalOutput").ap()
        dbg_m8 = nc.dram_tensor("dbg_m8", [128, 16 * 8], F32, kind="ExternalOutput").ap()
        dbg_rc = nc.dram_tensor("dbg_rc", [128, 32], F32, kind="ExternalOutput").ap()
    qk_out = nc.dram_tensor("qk_out", [2 * D, TOK], BF16, kind="ExternalOutput").ap()
    v_out = nc.dram_tensor("v_out", [TOK, D], BF16, kind="ExternalOutput").ap()

    st0 = contextlib.ExitStack()
    t_ = alloc(nc, st0, sbufs=[
        ("BIGA", (128, 16, 1024), F32),
        ("ones", (128, 128), F32),
        ("idn", (128, 128), F32),
        ("consts", (128, 4), F32),
        ("gb", (128, 16), F32),
        ("tmp", (128, 512), F32),
        ("cwsb", (128, 128), F32),
        ("glog", (128, 128), F32),
        ("m8", (128, 8), F32),
        ("mcol", (128, 8), F32),
        ("mcol16", (128, 16), F32),
        ("m2col", (128, 16), F32),
        ("rcol16", (128, 16), F32),
        ("m8big", (128, 16, 8), F32),
    ], psums=[(f"ps{i}", (128, 512), F32) for i in range(8)],
       sems=["tick", "dm_sync", "dm_gp", "dm_sc"])
    BIGA, ones, idn, consts = t_["BIGA"], t_["ones"], t_["idn"], t_["consts"]
    gb, tmp = t_["gb"], t_["tmp"]
    cwsb, glog, m8, mcol = t_["cwsb"], t_["glog"], t_["m8"], t_["mcol"]
    mcol16, rcol16, m8big = t_["mcol16"], t_["rcol16"], t_["m8big"]
    m2col = t_["m2col"]
    ps = [t_[f"ps{i}"] for i in range(8)]
    tick, dm_sync, dm_gp, dm_sc = t_["tick"], t_["dm_sync"], t_["dm_gp"], t_["dm_sc"]

    with nc.Block() as block:
        prog = Prog(nc, block, tick, {"sync": dm_sync, "gpsimd": dm_gp, "scalar": dm_sc})
        dma_s = prog.dmas("sync")
        dma_g = prog.dmas("gpsimd")

        st_xn1 = contextlib.ExitStack()
        xn1 = alloc(nc, st_xn1, sbufs=[("xn1", (128, 8, TOK), BF16)])["xn1"]
        stA = contextlib.ExitStack()
        sB = alloc(nc, stA, sbufs=[("s", (128, 8, TOK), F32)])["s"]

        stA1 = contextlib.ExitStack()
        tA1 = alloc(nc, stA1, sbufs=[
            ("outwc", (128, 16, 256), BF16),
            ("yb", (128, 16, 512), BF16),
            ("xst2", (128, 2, 512), F32),
        ])
        outwc, yb, xst2 = tA1["outwc"], tA1["yb"], tA1["xst2"]

        prog.step(sync=lambda e: dma_s(e, [
            (gb[:, 0:8], gamma), (gb[:, 8:16], beta), (idn[:], ident),
        ]), vector=lambda v: (v.memset(ones[:], 1.0),
                              v.memset(consts[:, 0:1], EPS))[-1])

        for n in range(4):
            n_sl = slice(n * 512, (n + 1) * 512)
            prog.step(sync=lambda e, n_sl=n_sl: dma_s(e, [
                (yb[:], y_in[:, n_sl].rearrange("(k p) t -> p k t", p=128)),
            ]))
            for mw in range(4):
                def ldw(g, mw=mw, n_sl=n_sl):
                    return dma_g(g, [
                        (outwc[:], outwT[:, mw * 256:(mw + 1) * 256]
                         .rearrange("(k p) m -> p k m", p=128)),
                        (xst2[:], x_in[:, n_sl]
                         .rearrange("(j p) t -> p j t", p=128)[:, 2 * mw:2 * mw + 2, :]),
                    ])
                prog.step(gpsimd=ldw)

                def mms(t):
                    last = None
                    for mi in range(2):
                        for k in range(16):
                            last = t.matmul(ps[mi][:], outwc[:, k, mi * 128:(mi + 1) * 128],
                                            yb[:, k, :], start=(k == 0), stop=(k == 15))
                    return last

                def evac(v, mw=mw, n_sl=n_sl):
                    last = None
                    for mi in range(2):
                        m = mw * 2 + mi
                        last = v.scalar_tensor_tensor(sB[:, m, n_sl], xst2[:, mi, :], 2.0,
                                                      ps[mi][:], ALU.mult, ALU.add)
                    return last
                prog.step(tensor=mms)
                prog.step(vector=evac)
        stA1.close()

        stS = contextlib.ExitStack()
        tS = alloc(nc, stS, sbufs=[
            ("sq", (128, 2, 512), F32),
            ("rows", (1, 4 * TOK), F32),
            ("mv", (1, 2 * 512), F32),
        ])
        sq, rows, mv = tS["sq"], tS["rows"], tS["mv"]
        a1row = rows[:, 0 * TOK:1 * TOK]
        b1row = rows[:, 1 * TOK:2 * TOK]
        a2row = rows[:, 2 * TOK:3 * TOK]
        b2row_s = rows[:, 3 * TOK:4 * TOK]

        emit_ln_stats(prog, lambda j, n_sl: sB[:, j, n_sl], sq, ps[0], ps[1],
                      ones, mv, a1row, b1row, consts)

        for n in range(4):
            n_sl = slice(n * 512, (n + 1) * 512)

            def bc1(t, n_sl=n_sl):
                t.matmul(ps[2][:], ones[0:1, 0:128], a1row[0:1, n_sl], start=True, stop=True)
                return t.matmul(ps[3][:], ones[0:1, 0:128], b1row[0:1, n_sl],
                                start=True, stop=True)
            prog.step(tensor=bc1)
            for g0 in range(0, 8, 2):
                def x1mk(v, g0=g0, n_sl=n_sl):
                    last = None
                    for i in range(2):
                        v.tensor_tensor(sq[:, i, :], sB[:, g0 + i, n_sl], ps[2][:], ALU.mult)
                        last = v.tensor_tensor(sq[:, i, :], sq[:, i, :], ps[3][:], ALU.add)
                    return last
                prog.step(vector=x1mk)

                def mmx1(t, g0=g0):
                    last = None
                    for i in range(2):
                        j = g0 + i
                        last = t.matmul(ps[0][0:1, :], ones[:, 0:1], sq[:, i, :],
                                        start=(j == 0), stop=(j == 7))
                    return last
                prog.step(tensor=mmx1)

                def sqx1(s):
                    last = None
                    for i in range(2):
                        last = s.activation(sq[:, i, :], sq[:, i, :], AF.Square)
                    return last
                prog.step(scalar=sqx1)

                def mmq1(t, g0=g0):
                    last = None
                    for i in range(2):
                        j = g0 + i
                        last = t.matmul(ps[1][0:1, :], ones[:, 0:1], sq[:, i, :],
                                        start=(j == 0), stop=(j == 7))
                    return last
                prog.step(tensor=mmq1)

            def st2(v):
                v.tensor_scalar(mv[0:1, 0:512], ps[0][0:1, :], 1.0 / D, None, ALU.mult)
                v.tensor_scalar(mv[0:1, 512:1024], ps[1][0:1, :], 1.0 / D, None, ALU.mult)
                v.tensor_tensor(tmp[0:1, 0:512], mv[0:1, 0:512], mv[0:1, 0:512], ALU.mult)
                return v.tensor_tensor(mv[0:1, 512:1024], mv[0:1, 512:1024],
                                       tmp[0:1, 0:512], ALU.subtract)
            prog.step(vector=st2)

            def rstd2(s):
                s.activation(mv[0:1, 512:1024], mv[0:1, 512:1024], AF.Ln,
                             bias=consts[0:1, 0:1])
                return s.activation(mv[0:1, 512:1024], mv[0:1, 512:1024],
                                    AF.Exp, scale=-0.5)
            prog.step(scalar=rstd2)

            def rows2(v, n_sl=n_sl):
                v.tensor_copy(a2row[0:1, n_sl], mv[0:1, 512:1024])
                return v.scalar_tensor_tensor(b2row_s[0:1, n_sl], mv[0:1, 0:512], -1.0,
                                              mv[0:1, 512:1024], ALU.mult, ALU.mult)
            prog.step(vector=rows2)

        if dbg:
            prog.step(sync=lambda e: dma_s(e, [
                (dbg_s.rearrange("(j p) t -> p j t", p=128), sB[:]),
                (dbg_rows, rows[:]),
            ]))
        for n in range(4):
            n_sl = slice(n * 512, (n + 1) * 512)

            def bc2(t, n_sl=n_sl):
                t.matmul(ps[2][:], ones[0:1, 0:128], a1row[0:1, n_sl], start=True, stop=True)
                t.matmul(ps[3][:], ones[0:1, 0:128], b1row[0:1, n_sl], start=True, stop=True)
                t.matmul(ps[4][:], ones[0:1, 0:128], a2row[0:1, n_sl], start=True, stop=True)
                return t.matmul(ps[5][:], ones[0:1, 0:128], b2row_s[0:1, n_sl],
                                start=True, stop=True)
            prog.step(tensor=bc2)

            for j in range(8):
                def vx(v, j=j, n_sl=n_sl):
                    v.tensor_tensor(tmp[:], sB[:, j, n_sl], ps[2][:], ALU.mult)
                    v.tensor_tensor(tmp[:], tmp[:], ps[3][:], ALU.add)
                    v.tensor_tensor(sq[:, 0, :], tmp[:], ps[4][:], ALU.mult)
                    v.tensor_tensor(sq[:, 0, :], sq[:, 0, :], ps[5][:], ALU.add)
                    return v.tensor_scalar(xn1[:, j, n_sl], sq[:, 0, :],
                                           gb[:, j:j + 1], gb[:, 8 + j:9 + j],
                                           ALU.mult, ALU.add)
                prog.step(vector=vx)

                for h in range(2):
                    def tr(t, h=h):
                        t.transpose(ps[6][:, 0:128], tmp[:, 256 * h:256 * h + 128], idn[:])
                        return t.transpose(ps[7][:, 0:128],
                                           tmp[:, 256 * h + 128:256 * h + 256], idn[:])
                    prog.step(tensor=tr)

                    def ev(v, j=j, n=n, h=h):
                        tau0 = n * 4 + 2 * h
                        v.tensor_scalar(BIGA[:, tau0, j * 128:(j + 1) * 128],
                                        ps[6][:, 0:128], 2.0, None, ALU.mult)
                        return v.tensor_scalar(BIGA[:, tau0 + 1, j * 128:(j + 1) * 128],
                                               ps[7][:, 0:128], 2.0, None, ALU.mult)
                    prog.step(vector=ev)
        if dbg:
            prog.step(sync=lambda e: dma_s(e, [
                (dbg_xn1.rearrange("(j p) t -> p j t", p=128), xn1[:]),
                (dbg_acc, BIGA[:].rearrange("p a b -> p (a b)")),
            ]))
        stS.close()
        stA.close()

        stG = contextlib.ExitStack()
        gwb = alloc(nc, stG, sbufs=[("gwb", (128, 8, 8), BF16)])["gwb"]
        prog.step(gpsimd=lambda g: dma_g(g, [
            (gwb[:], gate_w.rearrange("(k p) e -> p k e", p=128)),
        ]))

        def gmm(t):
            last = None
            for tau in range(16):
                for k in range(8):
                    last = t.matmul(ps[0][:, tau * 8:(tau + 1) * 8],
                                    xn1[:, k, tau * 128:(tau + 1) * 128],
                                    gwb[:, k, :], start=(k == 0), stop=(k == 7))
            return last
        prog.step(tensor=gmm)

        def g_mx(v):
            last = None
            for tau in range(16):
                last = v.tensor_reduce(m8[:, tau:tau + 1] if False else glog[:, tau, 0:1],
                                       ps[0][:, tau * 8:(tau + 1) * 8],
                                       mybir.AxisListType.X, ALU.max)
            return last
        prog.step(vector=lambda v: [v.tensor_reduce(mcol16[:, tau:tau + 1],
                                                    ps[0][:, tau * 8:(tau + 1) * 8],
                                                    mybir.AxisListType.X, ALU.max)
                                    for tau in range(16)][-1])

        prog.step(vector=lambda v: [v.tensor_scalar(glog[:, tau * 8:(tau + 1) * 8],
                                                    ps[0][:, tau * 8:(tau + 1) * 8],
                                                    mcol16[:, tau:tau + 1], None,
                                                    ALU.subtract)
                                    for tau in range(16)][-1])
        prog.step(scalar=lambda s: s.activation(glog[:], glog[:], AF.Exp))
        if dbg:
            prog.step(sync=lambda e: dma_s(e, [
                (dbg_g2, glog[:])]))
        prog.step(vector=lambda v: [v.max(out=m8big[:, tau, :],
                                           in_=glog[:, tau * 8:(tau + 1) * 8])
                                    for tau in range(16)][-1])
        prog.step(vector=lambda v: [v.tensor_copy(m2col[:, tau:tau + 1],
                                                  m8big[:, tau, 1:2])
                                    for tau in range(16)][-1])
        prog.step(vector=lambda v: [v.tensor_tensor(mcol16[:, tau:tau + 1],
                                                    m8big[:, tau, 0:1],
                                                    m8big[:, tau, 1:2], ALU.add)
                                    for tau in range(16)][-1])
        prog.step(vector=lambda v: v.reciprocal(rcol16[:], mcol16[:]))
        if dbg:
            prog.step(sync=lambda e: dma_s(e, [
                (dbg_m8, m8big[:].rearrange("p a b -> p (a b)")),
                (dbg_rc[:, 0:16], mcol16[:]), (dbg_rc[:, 16:32], rcol16[:])]))
        prog.step(vector=lambda v: [v.tensor_scalar(cwsb[:, tau * 8:(tau + 1) * 8],
                                                    glog[:, tau * 8:(tau + 1) * 8],
                                                    m2col[:, tau:tau + 1], None, ALU.is_ge)
                                    for tau in range(16)][-1])
        prog.step(vector=lambda v: v.tensor_tensor(cwsb[:], cwsb[:], glog[:], ALU.mult))
        prog.step(vector=lambda v: [v.tensor_scalar(cwsb[:, tau * 8:(tau + 1) * 8],
                                                    cwsb[:, tau * 8:(tau + 1) * 8],
                                                    rcol16[:, tau:tau + 1], None, ALU.mult)
                                    for tau in range(16)][-1])
        stG.close()

        stM = contextlib.ExitStack()
        tM = alloc(nc, stM, sbufs=[
            ("hbf", (128, 32, 1024), BF16),
            ("w1c", (128, 2, 8, 128), BF16),
            ("w2w", (128, 2, 8, 512), BF16),
            ("b1s", (128, E * 32), F32),
            ("b2t", (1, D), F32),
        ])
        hbf, w1c, w2w, b1s, b2t = tM["hbf"], tM["w1c"], tM["w2w"], tM["b1s"], tM["b2t"]
        prog.step(sync=lambda e: dma_s(e, [(b1s[:], b1c)]))

        for e in range(E):
            prog.step(sync=lambda e_=None, e2=e: None if e2 is None else None,
                      gpsimd=lambda g, e2=e: dma_g(g, [
                          (b2t[:], b2row[0:1, e2 * D:(e2 + 1) * D])]))
            for nn in range(2):
                tok0 = nn * 1024
                prog.step(gpsimd=lambda g, e2=e: dma_g(g, [
                    (w1c[:, 0], w1[e2, :, 0:128].rearrange("(k p) m -> p k m", p=128)),
                ]))
                for hm in range(33):
                    def work(t, hm=hm, tok0=tok0):
                        if hm >= 32:
                            return None
                        last = None
                        for th in range(2):
                            for k in range(8):
                                last = t.matmul(ps[(hm % 2) * 2 + th][:],
                                                w1c[:, hm % 2, k, :],
                                                xn1[:, k, tok0 + th * 512:tok0 + (th + 1) * 512],
                                                start=(k == 0), stop=(k == 7))
                        return last

                    def pre(g, e2=e, hm=hm):
                        if hm + 1 >= 32:
                            return None
                        return dma_g(g, [(w1c[:, (hm + 1) % 2],
                                          w1[e2, :, (hm + 1) * 128:(hm + 2) * 128]
                                          .rearrange("(k p) m -> p k m", p=128))])

                    def gel(s, e2=e, hm=hm):
                        if hm == 0:
                            return None
                        hp = hm - 1
                        s.activation(hbf[:, hp, 0:512], ps[(hp % 2) * 2][:], AF.Gelu,
                                     bias=b1s[:, e2 * 32 + hp:e2 * 32 + hp + 1])
                        return s.activation(hbf[:, hp, 512:1024], ps[(hp % 2) * 2 + 1][:],
                                            AF.Gelu, bias=b1s[:, e2 * 32 + hp:e2 * 32 + hp + 1])
                    prog.step(tensor=work, gpsimd=pre, scalar=gel)

                for nd in range(2):
                    prog.step(gpsimd=lambda g, e2=e, nd=nd: dma_g(g, [
                        (w2w[:, 0], w2[e2, 0:1024, nd * 512:(nd + 1) * 512]
                         .rearrange("(k p) m -> p k m", p=128)),
                    ]))
                    for kw in range(4):
                        def mm2(t, kw=kw):
                            last = None
                            for ki in range(8):
                                k = kw * 8 + ki
                                for tm in range(8):
                                    last = t.matmul(ps[tm][:],
                                                    hbf[:, k, tm * 128:(tm + 1) * 128],
                                                    w2w[:, kw % 2, ki, :],
                                                    start=(k == 0), stop=False)
                            return last

                        def pw2(g, e2=e, kw=kw, nd=nd):
                            if kw + 1 >= 4:
                                return None
                            k0 = (kw + 1) * 1024
                            return dma_g(g, [(w2w[:, (kw + 1) % 2],
                                              w2[e2, k0:k0 + 1024, nd * 512:(nd + 1) * 512]
                                              .rearrange("(k p) m -> p k m", p=128))])
                        prog.step(tensor=mm2, gpsimd=pw2)

                    def b2mm(t, nd=nd):
                        last = None
                        for tm in range(8):
                            last = t.matmul(ps[tm][:], ones[0:1, 0:128],
                                            b2t[0:1, nd * 512:(nd + 1) * 512],
                                            start=False, stop=True)
                        return last
                    prog.step(tensor=b2mm)

                    def moevac(v, e2=e, nn=nn, nd=nd):
                        last = None
                        for tm in range(8):
                            tau = nn * 8 + tm
                            last = v.scalar_tensor_tensor(
                                BIGA[:, tau, nd * 512:(nd + 1) * 512], ps[tm][:],
                                cwsb[:, tau * 8 + e2:tau * 8 + e2 + 1],
                                BIGA[:, tau, nd * 512:(nd + 1) * 512], ALU.mult, ALU.add)
                        return last
                    prog.step(vector=moevac)
        if dbg:
            prog.step(sync=lambda e: dma_s(e, [
                (dbg_acc2, BIGA[:].rearrange("p a b -> p (a b)")),
            ]))
        stM.close()
        st_xn1.close()

        stQ = contextlib.ExitStack()
        tQ = alloc(nc, stQ, sbufs=[
            ("x2bf", (128, 8, TOK), BF16),
            ("wat", (128, 8, 3 * D), BF16),
            ("tmq", (128, 1024), F32),
            ("stg", (128, 8, 128), F32),
            ("stgb", (128, 4, 512), BF16),
        ])
        x2bf, wat, tmq, stg, stgb = tQ["x2bf"], tQ["wat"], tQ["tmq"], tQ["stg"], tQ["stgb"]
        prog.step(gpsimd=lambda g: dma_g(g, [
            (wat[:], wattnT.rearrange("(k p) m -> p k m", p=128)),
        ]))

        for tau in range(16):
            def lnst(s, tau=tau):
                return s.activation(tmq[:], BIGA[:, tau, :], AF.Square,
                                    accum_out=mcol[:, 1:2])
            prog.step(scalar=lnst)

            def lnred(v, tau=tau):
                v.tensor_reduce(mcol[:, 0:1], BIGA[:, tau, :], mybir.AxisListType.X, ALU.add)
                v.tensor_scalar(mcol[:, 0:1], mcol[:, 0:1], 1.0 / D, None, ALU.mult)
                v.tensor_scalar(mcol[:, 1:2], mcol[:, 1:2], 1.0 / D, None, ALU.mult)
                v.tensor_tensor(mcol[:, 2:3], mcol[:, 0:1], mcol[:, 0:1], ALU.mult)
                v.tensor_tensor(mcol[:, 1:2], mcol[:, 1:2], mcol[:, 2:3], ALU.subtract)
                return v.tensor_scalar(mcol[:, 3:4], mcol[:, 0:1], -1.0, None, ALU.mult)
            prog.step(vector=lnred)

            def lnrs(s):
                s.activation(mcol[:, 1:2], mcol[:, 1:2], AF.Ln, bias=consts[:, 0:1])
                return s.activation(mcol[:, 1:2], mcol[:, 1:2], AF.Exp, scale=-0.5)
            prog.step(scalar=lnrs)

            def lnap(v, tau=tau):
                return v.tensor_scalar(tmq[:], BIGA[:, tau, :], mcol[:, 3:4],
                                       mcol[:, 1:2], ALU.add, ALU.mult)
            prog.step(vector=lnap)

            for h in range(4):
                def trx(t, h=h):
                    t.transpose(ps[6][:, 0:128], tmq[:, 256 * h:256 * h + 128], idn[:])
                    return t.transpose(ps[7][:, 0:128],
                                       tmq[:, 256 * h + 128:256 * h + 256], idn[:])
                prog.step(tensor=trx)

                def evx(v, h=h, tau=tau):
                    v.tensor_copy(stg[:, 2 * h, :], ps[6][:, 0:128])
                    v.tensor_copy(stg[:, 2 * h + 1, :], ps[7][:, 0:128])
                    v.tensor_copy(x2bf[:, 2 * h, tau * 128:(tau + 1) * 128], ps[6][:, 0:128])
                    return v.tensor_copy(x2bf[:, 2 * h + 1, tau * 128:(tau + 1) * 128],
                                         ps[7][:, 0:128])
                prog.step(vector=evx)

            prog.step(sync=lambda e, tau=tau: dma_s(e, [
                (x2_out[:, tau * 128:(tau + 1) * 128]
                 .rearrange("(j p) t -> p j t", p=128), stg[:])]))

        for n in range(4):
            n_sl = slice(n * 512, (n + 1) * 512)
            for mw in range(4):
                def qmm(t, mw=mw, n_sl=n_sl):
                    last = None
                    for mi in range(4):
                        m = mw * 4 + mi
                        for k in range(8):
                            last = t.matmul(ps[mi][:], wat[:, k, m * 128:(m + 1) * 128],
                                            x2bf[:, k, n_sl], start=(k == 0), stop=(k == 7))
                    return last
                prog.step(tensor=qmm)

                def qev(v):
                    last = None
                    for mi in range(4):
                        last = v.tensor_copy(stgb[:, mi, :], ps[mi][:])
                    return last
                prog.step(vector=qev)

                prog.step(sync=lambda e, mw=mw, n_sl=n_sl: dma_s(e, [
                    (qk_out[mw * 512:(mw + 1) * 512, n_sl]
                     .rearrange("(m p) t -> p m t", p=128), stgb[:, :, :])]))

        for tw in range(4):
            for nv in range(2):
                def vmm(t, tw=tw, nv=nv):
                    last = None
                    for ti in range(4):
                        tau = tw * 4 + ti
                        for k in range(8):
                            last = t.matmul(ps[ti][:], x2bf[:, k, tau * 128:(tau + 1) * 128],
                                            wat[:, k, 2048 + nv * 512:2048 + (nv + 1) * 512],
                                            start=(k == 0), stop=(k == 7))
                    return last
                prog.step(tensor=vmm)

                def vev(v):
                    last = None
                    for ti in range(4):
                        last = v.tensor_copy(stgb[:, ti, :], ps[ti][:])
                    return last
                prog.step(vector=vev)

                prog.step(sync=lambda e, tw=tw, nv=nv: dma_s(e, [
                    (v_out[tw * 512:(tw + 1) * 512, nv * 512:(nv + 1) * 512]
                     .rearrange("(m p) t -> p m t", p=128), stgb[:, :, :])]))
        stQ.close()
    st0.close()
    return nc



def build_L5(nc):
    q_in = nc.dram_tensor("q_p", [4, HS, S], BF16, kind="ExternalInput").ap()
    k_in = nc.dram_tensor("k_p", [4, HS, S], BF16, kind="ExternalInput").ap()
    v_in = nc.dram_tensor("v_p", [4, S, HS], BF16, kind="ExternalInput").ap()
    dmask = nc.dram_tensor("dmask", [4 * 128, 512], F32, kind="ExternalInput").ap()
    y_out = nc.dram_tensor("y_p", [4 * HS, S], BF16, kind="ExternalOutput").ap()

    st0 = contextlib.ExitStack()
    t_ = alloc(nc, st0, sbufs=[
        ("q2", (128, S), BF16), ("k2", (128, S), BF16),
        ("vA", (128, 64, 65), BF16), ("vB", (128, 64, 65), BF16),
        ("msk", (128, 4, 512), F32),
        ("onesb", (128, 2), BF16),
        ("consts", (128, 2), F32),
        ("ebf", (128, 2, 2, 512), BF16),
        ("etmp", (128, 2, 2, 512), F32),
        ("onesT", (65, 64), F32),
        ("sufA", (65, 16), F32), ("sufB", (65, 16), F32),
        ("pref", (65, 64), F32), ("tcol", (65, 1), F32),
        ("yv", (65, 2, 512), F32),
        ("rrow", (1, 2, 512), F32),
        ("ystg", (64, 2, S), BF16),
    ], psums=[("psA0", (128, 512), F32), ("psA1", (128, 512), F32),
              ("psB0", (128, 512), F32), ("psB1", (128, 512), F32),
              ("pyA", (65, 512), F32), ("pyB", (65, 512), F32),
              ("pxt", (128, 512), F32)],
       sems=["tick", "dm_sync", "dm_gp"])
    q2, k2, vA, vB, msk = t_["q2"], t_["k2"], t_["vA"], t_["vB"], t_["msk"]
    onesb, consts, ebf, etmp, onesT = (t_["onesb"], t_["consts"], t_["ebf"],
                                       t_["etmp"], t_["onesT"])
    sufA, sufB, pref, tcol = t_["sufA"], t_["sufB"], t_["pref"], t_["tcol"]
    yv, rrow, ystg = t_["yv"], t_["rrow"], t_["ystg"]
    psS = [[t_["psA0"], t_["psA1"]], [t_["psB0"], t_["psB1"]]]
    pyP = [t_["pyA"], t_["pyB"]]
    pxt = t_["pxt"]
    tick, dm_sync, dm_gp = t_["tick"], t_["dm_sync"], t_["dm_gp"]

    with nc.Block() as block:
        prog = Prog(nc, block, tick, {"sync": dm_sync, "gpsimd": dm_gp})
        dma_s = prog.dmas("sync")

        prog.step(sync=lambda e: dma_s(e, [
            (msk[:], dmask.rearrange("(r p) f -> p r f", p=128)),
        ]), vector=lambda v: (v.memset(onesT[:], 1.0), v.memset(vA[:], 1.0),
                              v.memset(vB[:], 1.0), v.memset(onesb[:], 1.0),
                              v.memset(consts[:, 0:1], -10.0))[-1])

        for pg in range(2):
            pA, pB = 2 * pg, 2 * pg + 1
            prog.step(sync=lambda e, pA=pA, pB=pB: dma_s(e, [
                (q2[0:64, :], q_in[pA]), (q2[64:128, :], q_in[pB]),
                (k2[0:64, :], k_in[pA]), (k2[64:128, :], k_in[pB]),
                (vA[:, :, 0:64], v_in[pA].rearrange("(kt p) c -> p kt c", p=128)),
                (vB[:, :, 0:64], v_in[pB].rearrange("(kt p) c -> p kt c", p=128)),
            ]))
            for pi, vt, suf in ((0, vA, sufA), (1, vB, sufB)):
                def tsum(t, vt=vt):
                    last = None
                    for kt in range(64):
                        last = t.matmul(pxt[0:65, kt:kt + 1], vt[:, kt, :],
                                        onesb[:, 0:1], start=True, stop=True)
                    return last
                prog.step(tensor=tsum)
                prog.step(vector=lambda v: v.tensor_tensor_scan(
                    pref[:], onesT[:], pxt[0:65, 0:64], 0.0, ALU.mult, ALU.add))
                prog.step(vector=lambda v: v.tensor_scalar(
                    tcol[:], pref[:, 63:64], EM10, None, ALU.mult))
                prog.step(vector=lambda v, suf=suf: v.tensor_scalar(
                    suf[:], pref[:, 3:64:4], -EM10, tcol[:, 0:1], ALU.mult, ALU.add))

            for qt in range(16):
                q_sl = slice(qt * 512, (qt + 1) * 512)
                nkt = 4 * qt + 4
                kts = list(range(nkt))
                waves = [kts[i:i + 2] for i in range(0, nkt, 2)]
                for w, wk in enumerate(waves):
                    diag = (wk[0] >= 4 * qt)

                    def smm(t, wk=wk, q_sl=q_sl):
                        last = None
                        for si, kt in enumerate(wk):
                            k_sl = slice(kt * 128, (kt + 1) * 128)
                            t.matmul(psS[0][si][:], k2[0:64, k_sl], q2[0:64, q_sl],
                                     start=True, stop=True, tile_position=(0, 0))
                            last = t.matmul(psS[1][si][:], k2[64:128, k_sl],
                                            q2[64:128, q_sl], start=True, stop=True,
                                            tile_position=(64, 0))
                        return last
                    prog.step(tensor=smm)

                    if diag:
                        def dmsk(v, wk=wk, qt=qt):
                            last = None
                            for si, kt in enumerate(wk):
                                rel = kt - 4 * qt
                                for pi in range(2):
                                    v.tensor_scalar(etmp[:, pi, si, :], psS[pi][si][:],
                                                    0.125, 10.0, ALU.mult, ALU.add)
                                    last = v.tensor_tensor(etmp[:, pi, si, :],
                                                           etmp[:, pi, si, :],
                                                           msk[:, rel, :], ALU.mult)
                            return last
                        prog.step(vector=dmsk)

                        def dexp(s, wk=wk):
                            last = None
                            for si in range(len(wk)):
                                for pi in range(2):
                                    last = s.activation(ebf[:, pi, si, :],
                                                        etmp[:, pi, si, :], AF.Exp,
                                                        bias=consts[:, 0:1])
                            return last
                        prog.step(scalar=dexp)
                    else:
                        def nexp(s, wk=wk):
                            last = None
                            for si in range(len(wk)):
                                for pi in range(2):
                                    last = s.activation(ebf[:, pi, si, :],
                                                        psS[pi][si][:], AF.Exp,
                                                        scale=0.125)
                            return last
                        prog.step(scalar=nexp)

                    def avm(t, wk=wk):
                        last = None
                        for si, kt in enumerate(wk):
                            for pi, vt in ((0, vA), (1, vB)):
                                last = t.matmul(pyP[pi][:], vt[:, kt, :],
                                                ebf[:, pi, si, :],
                                                start=(kt == 0), stop=(kt == nkt - 1))
                        return last
                    prog.step(tensor=avm)

                def ev1(v, qt=qt):
                    v.tensor_scalar(yv[:, 0, :], pyP[0][:], sufA[:, qt:qt + 1],
                                    None, ALU.add)
                    return v.tensor_scalar(yv[:, 1, :], pyP[1][:], sufB[:, qt:qt + 1],
                                           None, ALU.add)
                prog.step(vector=ev1)

                def ev2(v):
                    v.reciprocal(rrow[0:1, 0, :], yv[64:65, 0, :])
                    return v.reciprocal(rrow[0:1, 1, :], yv[64:65, 1, :])
                prog.step(vector=ev2)

                for pi in range(2):
                    prog.step(tensor=lambda t, pi=pi: t.matmul(
                        pxt[0:64, :], onesT[0:1, 0:64], rrow[0:1, pi, :],
                        start=True, stop=True))
                    prog.step(vector=lambda v, pi=pi, q_sl=q_sl: v.tensor_tensor(
                        ystg[:, pi, q_sl], yv[0:64, pi, :], pxt[0:64, :], ALU.mult))

            prog.step(sync=lambda e, pA=pA, pB=pB: dma_s(e, [
                (y_out[pA * 64:(pA + 1) * 64, :], ystg[:, 0, :]),
                (y_out[pB * 64:(pB + 1) * 64, :], ystg[:, 1, :]),
            ]))
    st0.close()
    return nc



def build_L6(nc):
    yat = nc.dram_tensor("yat_cols", [D, TOK], BF16, kind="ExternalInput").ap()
    x2_in = nc.dram_tensor("x2_fm", [D, TOK], F32, kind="ExternalInput").ap()
    wprojT = nc.dram_tensor("wprojT", [D, D], F32, kind="ExternalInput").ap()
    Wb = nc.dram_tensor("W_base", [D, D], F32, kind="ExternalInput").ap()
    bbase = nc.dram_tensor("bbase", [128, 8], F32, kind="ExternalInput").ap()
    Wd1 = nc.dram_tensor("Wd1", [E, D, 256], F32, kind="ExternalInput").ap()
    bd1c = nc.dram_tensor("bd1c", [128, 16], F32, kind="ExternalInput").ap()
    Wd2 = nc.dram_tensor("Wd2", [E, 256, D], F32, kind="ExternalInput").ap()
    bd2c = nc.dram_tensor("bd2c", [128, 64], F32, kind="ExternalInput").ap()
    x3_out = nc.dram_tensor("x3_out", [D, TOK], F32, kind="ExternalOutput").ap()
    x3b_out = nc.dram_tensor("x3b_out", [D, TOK], BF16, kind="ExternalOutput").ap()
    bx3_out = nc.dram_tensor("bx3_out", [D, TOK], F32, kind="ExternalOutput").ap()
    loss_out = nc.dram_tensor("loss_out", [1, E], F32, kind="ExternalOutput").ap()

    st0 = contextlib.ExitStack()
    t_ = alloc(nc, st0, sbufs=[
        ("sx", (128, 8, TOK), F32),
        ("x3b", (128, 8, TOK), BF16),
        ("bx3", (128, 8, TOK), F32),
        ("ones", (128, 128), F32),
        ("consts", (128, 4), F32),
        ("tmp", (128, 512), F32),
        ("wcast", (128, 8, 512), BF16),
        ("xs2", (128, 4, 512), F32),
        ("sq", (128, 1, 512), F32),
        ("mv", (1, 4 * 512), F32),
        ("bcols", (128, 9), F32),
        ("lcols", (128, 33), F32),
        ("lrow", (1, 8), F32),
        ("h1", (128, 2, 512), BF16),
        ("wd1c", (128, 8, 256), BF16),
        ("wd2c", (128, 2, 512), BF16),
        ("bd1s", (128, 16), F32),
        ("bd2s", (128, 64), F32),
        ("yb", (128, 8, 512), BF16),
    ], psums=[(f"ps{i}", (128, 512), F32) for i in range(8)],
       sems=["tick", "dm_sync", "dm_gp"])
    sx, x3b, bx3 = t_["sx"], t_["x3b"], t_["bx3"]
    ones, consts, tmp = t_["ones"], t_["consts"], t_["tmp"]
    wcast, xs2, sq, mv = t_["wcast"], t_["xs2"], t_["sq"], t_["mv"]
    bcols, lcols, lrow = t_["bcols"], t_["lcols"], t_["lrow"]
    h1, wd1c, wd2c, bd1s, bd2s, yb = (t_["h1"], t_["wd1c"], t_["wd2c"],
                                      t_["bd1s"], t_["bd2s"], t_["yb"])
    ps = [t_[f"ps{i}"] for i in range(8)]
    tick, dm_sync, dm_gp = t_["tick"], t_["dm_sync"], t_["dm_gp"]

    with nc.Block() as block:
        prog = Prog(nc, block, tick, {"sync": dm_sync, "gpsimd": dm_gp})
        dma_s = prog.dmas("sync")
        dma_g = prog.dmas("gpsimd")

        prog.step(sync=lambda e: dma_s(e, [
            (bcols[:, 0:8], bbase), (bd1s[:], bd1c), (bd2s[:], bd2c),
        ]), vector=lambda v: (v.memset(ones[:], 1.0),
                              v.memset(consts[:, 0:1], EPS))[-1])

        for n in range(4):
            n_sl = slice(n * 512, (n + 1) * 512)
            prog.step(sync=lambda e, n_sl=n_sl: dma_s(e, [
                (yb[:], yat[:, n_sl].rearrange("(k p) t -> p k t", p=128)),
            ]))
            for mw in range(2):
                prog.step(sync=lambda e, mw=mw, n_sl=n_sl: dma_s(e, [
                    (xs2[:], x2_in[:, n_sl].rearrange("(j p) t -> p j t", p=128)
                     [:, 4 * mw:4 * mw + 4, :]),
                ]), gpsimd=lambda g, mw=mw: dma_g(g, [
                    (wcast[:], wprojT[:, mw * 512:(mw + 1) * 512]
                     .rearrange("(k p) m -> p k m", p=128)),
                ]))

                def mms(t):
                    last = None
                    for mi in range(4):
                        for k in range(8):
                            last = t.matmul(ps[mi][:], wcast[:, k, mi * 128:(mi + 1) * 128],
                                            yb[:, k, :], start=(k == 0), stop=(k == 7))
                    return last

                def evac(v, mw=mw, n_sl=n_sl):
                    last = None
                    for mi in range(4):
                        m = mw * 4 + mi
                        last = v.scalar_tensor_tensor(sx[:, m, n_sl], xs2[:, mi, :], 1.0,
                                                      ps[mi][:], ALU.mult, ALU.add)
                    return last
                prog.step(tensor=mms)
                prog.step(vector=evac)

        def ln3_out(j, n, n_sl, v, x_ap, ps_a, ps_b):
            v.tensor_tensor(tmp[:], x_ap, ps_a[:], ALU.mult)
            v.tensor_tensor(tmp[:], tmp[:], ps_b[:], ALU.add)
            v.tensor_copy(x3b[:, j, n_sl], tmp[:])
            return v.tensor_copy(sx[:, j, n_sl], tmp[:])
        emit_ln_fused(prog, lambda j, n_sl: sx[:, j, n_sl], ln3_out, sq,
                      ps[0], ps[1], ps[2], ps[3], ones, mv, consts)
        prog.step(sync=lambda e: dma_s(e, [
            (x3_out.rearrange("(j p) t -> p j t", p=128), sx[:]),
            (x3b_out.rearrange("(j p) t -> p j t", p=128), x3b[:]),
        ]))

        for n in range(4):
            n_sl = slice(n * 512, (n + 1) * 512)
            for mw in range(2):
                prog.step(gpsimd=lambda g, mw=mw: dma_g(g, [
                    (wcast[:], Wb[:, mw * 512:(mw + 1) * 512]
                     .rearrange("(k p) m -> p k m", p=128)),
                ]))

                def mmb(t, mw=mw, n_sl=n_sl):
                    last = None
                    for mi in range(4):
                        for k in range(8):
                            last = t.matmul(ps[mi][:], wcast[:, k, mi * 128:(mi + 1) * 128],
                                            x3b[:, k, n_sl], start=(k == 0), stop=(k == 7))
                    return last

                def evb(v, mw=mw, n_sl=n_sl):
                    last = None
                    for mi in range(4):
                        m = mw * 4 + mi
                        v.tensor_scalar(tmp[:], ps[mi][:], bcols[:, m:m + 1], None, ALU.add)
                        last = v.tensor_tensor(bx3[:, m, n_sl], tmp[:], sx[:, m, n_sl],
                                               ALU.add)
                    return last
                prog.step(tensor=mmb)
                prog.step(vector=evb)
        prog.step(sync=lambda e: dma_s(e, [
            (bx3_out.rearrange("(j p) t -> p j t", p=128), bx3[:]),
        ]))

        for e_ in range(E):
            prog.step(gpsimd=lambda g, e_=e_: dma_g(g, [
                (wd1c[:], Wd1[e_].rearrange("(k p) m -> p k m", p=128)),
            ]))
            for n in range(4):
                n_sl = slice(n * 512, (n + 1) * 512)

                def mh(t, n_sl=n_sl):
                    last = None
                    for mi in range(2):
                        for k in range(8):
                            last = t.matmul(ps[6 - mi][:], wd1c[:, k, mi * 128:(mi + 1) * 128],
                                            x3b[:, k, n_sl], start=(k == 0), stop=(k == 7))
                    return last
                prog.step(tensor=mh)

                def eh(s_, e_=e_):
                    s_.activation(h1[:, 0, :], ps[6][:], AF.Relu,
                                  bias=bd1s[:, 2 * e_:2 * e_ + 1])
                    return s_.activation(h1[:, 1, :], ps[5][:], AF.Relu,
                                         bias=bd1s[:, 2 * e_ + 1:2 * e_ + 2])
                prog.step(scalar=eh)
                for mw in range(2):
                    prog.step(gpsimd=lambda g, e_=e_, mw=mw: dma_g(g, [
                        (wd2c[:], Wd2[e_][:, mw * 512:(mw + 1) * 512]
                         .rearrange("(k p) m -> p k m", p=128)),
                    ]))

                    def mr(t, mw=mw):
                        last = None
                        for mi in range(4):
                            for k in range(2):
                                last = t.matmul(ps[mi][:], wd2c[:, k, mi * 128:(mi + 1) * 128],
                                                h1[:, k, :], start=(k == 0), stop=(k == 1))
                        return last
                    prog.step(tensor=mr)

                    def dv2(v, mw=mw, n_sl=n_sl, e_=e_):
                        last = None
                        for mi in range(4):
                            m = mw * 4 + mi
                            last = v.scalar_tensor_tensor(
                                xs2[:, mi, :], ps[mi][:], bd2s[:, e_ * 8 + m:e_ * 8 + m + 1],
                                sx[:, m, n_sl], ALU.add, ALU.subtract)
                        return last
                    prog.step(vector=dv2)

                    def sqa(s_, n=n, mw=mw):
                        last = None
                        for mi in range(4):
                            ci = (n * 2 + mw) * 4 + mi
                            last = s_.activation(tmp[:], xs2[:, mi, :], AF.Square,
                                                 accum_out=lcols[:, ci:ci + 1])
                        return last
                    prog.step(scalar=sqa)
            prog.step(vector=lambda v: v.tensor_reduce(
                lcols[:, 32:33], lcols[:, 0:32], mybir.AxisListType.X, ALU.add))
            prog.step(tensor=lambda t: t.matmul(ps[4][0:1, 0:1], ones[:, 0:1],
                                                lcols[:, 32:33], start=True, stop=True))
            prog.step(vector=lambda v, e_=e_: v.tensor_copy(lrow[0:1, e_:e_ + 1],
                                                            ps[4][0:1, 0:1]))
        prog.step(sync=lambda e: dma_s(e, [(loss_out, lrow[:])]))
    st0.close()
    return nc



def build_L7(nc):
    lparts = nc.dram_tensor("lossparts", [E, E], F32, kind="ExternalInput").ap()
    selb = nc.dram_tensor("selb", [E, 2], F32, kind="ExternalInput").ap()
    x3b_in = nc.dram_tensor("x3b", [D, TOK], BF16, kind="ExternalInput").ap()
    bx3_in = nc.dram_tensor("bx3", [D, TOK], F32, kind="ExternalInput").ap()
    Wad = nc.dram_tensor("Wad", [E, D, 64], F32, kind="ExternalInput").ap()
    badc = nc.dram_tensor("badc", [64, E], F32, kind="ExternalInput").ap()
    Wau = nc.dram_tensor("Wau", [E, 64, D], F32, kind="ExternalInput").ap()
    baup8 = nc.dram_tensor("baup8", [E, D], F32, kind="ExternalInput").ap()
    out_d = nc.dram_tensor("out_slab", [D, TOK], F32, kind="ExternalOutput").ap()
    l2scr = nc.dram_tensor("l2scr", [2, E], F32, kind="Internal").ap()

    st0 = contextlib.ExitStack()
    t_ = alloc(nc, st0, sbufs=[
        ("bx3s", (128, 8, TOK), F32),
        ("x3bs", (128, 8, TOK), BF16),
        ("a_all", (64, 8, TOK), BF16),
        ("wauc", (64, 8, D), BF16),
        ("wadc", (128, 8, 64), BF16),
        ("lp", (8, 8), F32),
        ("sel_sb", (8, 2), F32),
        ("baups", (8, D), F32),
        ("badcs", (64, 8), F32),
        ("lrow", (1, 8), F32),
        ("mcol", (1, 2), F32),
        ("ind", (1, 8), F32),
        ("indc8", (8, 1), F32),
        ("ind64", (64, 8), F32),
        ("bcol", (128, 8), F32),
        ("ones", (128, 128), F32),
        ("consts", (128, 4), F32),
        ("tmp", (128, 512), F32),
        ("sq", (128, 2, 512), F32),
        ("mv", (1, 4 * 512), F32),
        ("ostg", (128, 8, 512), F32),
    ], psums=[(f"ps{i}", (128, 512), F32) for i in range(8)],
       sems=["tick", "dm_sync", "dm_gp"])
    bx3, x3b, a_all, wauc, wadc = (t_["bx3s"], t_["x3bs"], t_["a_all"], t_["wauc"],
                                   t_["wadc"])
    lp, sel_sb, baups, badcs = t_["lp"], t_["sel_sb"], t_["baups"], t_["badcs"]
    lrow, mcol, ind, indc8, ind64, bcol = (t_["lrow"], t_["mcol"], t_["ind"],
                                           t_["indc8"], t_["ind64"], t_["bcol"])
    ones, consts, tmp, sq, mv, ostg = (t_["ones"], t_["consts"], t_["tmp"],
                                       t_["sq"], t_["mv"], t_["ostg"])
    ps = [t_[f"ps{i}"] for i in range(8)]
    tick, dm_sync, dm_gp = t_["tick"], t_["dm_sync"], t_["dm_gp"]

    with nc.Block() as block:
        prog = Prog(nc, block, tick, {"sync": dm_sync, "gpsimd": dm_gp})
        dma_s = prog.dmas("sync")
        dma_g = prog.dmas("gpsimd")

        prog.step(sync=lambda e: dma_s(e, [
            (lp[:], lparts), (sel_sb[:], selb), (baups[:], baup8), (badcs[:], badc),
            (bx3[:], bx3_in.rearrange("(j p) t -> p j t", p=128)),
            (x3b[:], x3b_in.rearrange("(j p) t -> p j t", p=128)),
        ]), gpsimd=lambda g: dma_g(g, [
            (wauc[:], Wau.rearrange("e k m -> k e m")),
        ]), vector=lambda v: (v.memset(ones[:], 1.0),
                              v.memset(consts[:, 0:1], EPS))[-1])

        prog.step(tensor=lambda t: t.matmul(ps[0][0:2, 0:8], sel_sb[:], lp[:],
                                            start=True, stop=True))
        prog.step(vector=lambda v: v.tensor_copy(lp[0:2, :], ps[0][0:2, 0:8]))
        prog.step(sync=lambda e: dma_s(e, [(l2scr[:], lp[0:2, :])]))

        def ownrow(e):
            rank = e.partition_id()
            bc = rank // 4
            return dma_s(e, [(lrow[:], l2scr[bass.ts(bc, 1), :])])
        prog.step(sync=ownrow)

        prog.step(vector=lambda v: v.tensor_reduce(mcol[0:1, 0:1], lrow[:],
                                                   mybir.AxisListType.X, ALU.min))
        prog.step(vector=lambda v: v.tensor_scalar(ind[:], lrow[:], mcol[0:1, 0:1],
                                                   None, ALU.is_le))
        prog.step(tensor=lambda t: (
            t.matmul(ps[0][0:8, 0:1], ind[:], ones[0:1, 0:1], start=True, stop=True),
            t.matmul(ps[1][0:64, 0:8], ones[0:1, 0:64], ind[:], start=True, stop=True))[-1])
        prog.step(vector=lambda v: (v.tensor_copy(indc8[:], ps[0][0:8, 0:1]),
                                    v.tensor_copy(ind64[:], ps[1][0:64, 0:8]))[-1])
        prog.step(tensor=lambda t: [t.matmul(ps[2][0:128, m:m + 1],
                                             baups[:, m * 128:(m + 1) * 128],
                                             indc8[:], start=True, stop=True)
                                    for m in range(8)][-1])
        prog.step(vector=lambda v: v.tensor_copy(bcol[:], ps[2][:, 0:8]))

        for e_ in range(E):
            prog.step(gpsimd=lambda g, e_=e_: dma_g(g, [
                (wadc[:], Wad[e_].rearrange("(k p) m -> p k m", p=128)),
            ]))
            for n in range(4):
                def ma(t, n=n):
                    last = None
                    for k in range(8):
                        last = t.matmul(ps[0][0:64, :], wadc[:, k, :],
                                        x3b[:, k, n * 512:(n + 1) * 512],
                                        start=(k == 0), stop=(k == 7))
                    return last
                prog.step(tensor=ma)
                prog.step(scalar=lambda s, e_=e_, n=n: s.activation(
                    a_all[:, e_, n * 512:(n + 1) * 512], ps[0][0:64, :], AF.Relu,
                    bias=badcs[:, e_:e_ + 1]))
            prog.step(vector=lambda v, e_=e_: v.tensor_scalar(
                a_all[:, e_, :], a_all[:, e_, :], ind64[:, e_:e_ + 1], None, ALU.mult))

        for n in range(4):
            n_sl = slice(n * 512, (n + 1) * 512)
            for mw in range(2):
                def mu_(t, mw=mw, n_sl=n_sl):
                    last = None
                    for mi in range(4):
                        m = mw * 4 + mi
                        for e_ in range(E):
                            last = t.matmul(ps[mi][:], wauc[:, e_, m * 128:(m + 1) * 128],
                                            a_all[:, e_, n_sl],
                                            start=(e_ == 0), stop=(e_ == 7))
                    return last
                prog.step(tensor=mu_)

                def evu(v, mw=mw, n_sl=n_sl):
                    last = None
                    for mi in range(4):
                        m = mw * 4 + mi
                        v.tensor_scalar(tmp[:], ps[mi][:], bcol[:, m:m + 1], None, ALU.add)
                        last = v.tensor_tensor(bx3[:, m, n_sl], tmp[:], bx3[:, m, n_sl],
                                               ALU.add)
                    return last
                prog.step(vector=evu)

        def ln4_out(j, n, n_sl, v, x_ap, ps_a, ps_b):
            v.tensor_tensor(ostg[:, j, :], x_ap, ps_a[:], ALU.mult)
            return v.tensor_tensor(ostg[:, j, :], ostg[:, j, :], ps_b[:], ALU.add)
        def ln4_store(prog_, n, n_sl):
            prog_.step(sync=lambda e, n_sl=n_sl: dma_s(e, [
                (out_d[:, n_sl].rearrange("(j p) t -> p j t", p=128), ostg[:])]))
        emit_ln_fused(prog, lambda j, n_sl: bx3[:, j, n_sl], ln4_out, sq,
                      ps[0], ps[1], ps[2], ps[3], ones, mv, consts,
                      post_chunk=ln4_store)
    st0.close()
    return nc



def kernel(**I):
    P = host_prep(I)
    ident = np.eye(128, dtype=np.float32)
    cores = list(range(NC))

    nc = bass.Bass("TRN2", target_bir_lowering=False, debug=False)
    build_L1(nc)
    r1 = run_bass_kernel_spmd(nc, [
        {"x_fm": P["x_fm"][c], "lng": P["lng"], "lnb": P["lnb"]} for c in cores],
        core_ids=cores).results
    xn0_all = np.ascontiguousarray(np.concatenate(
        [np.asarray(r1[c]["xn0"]) for c in cores], axis=1))

    nc = bass.Bass("TRN2", target_bir_lowering=False, debug=False)
    build_L2(nc)
    r2 = run_bass_kernel_spmd(nc, [
        {"xn0_all": xn0_all, "inwT": P["inwT"][c], "convw": P["convw"][c],
         "convb": P["convb"][c], "xprojT": P["xprojT"][c]} for c in cores],
        core_ids=cores).results
    xp_parts = np.ascontiguousarray(np.stack(
        [np.asarray(r2[c]["xp_out"]) for c in cores]))

    nc = bass.Bass("TRN2", target_bir_lowering=False, debug=False)
    build_L3(nc)
    r3 = run_bass_kernel_spmd(nc, [
        {"xs_in": np.asarray(r2[c]["xs_out"]), "z_in": np.asarray(r2[c]["z_out"]),
         "xp_parts": xp_parts, "A_log": P["A_log"], "dtw": P["dtw"][c],
         "dtb": P["dtb"][c], "Dc": P["Dc"][c], "sel": P["sel"]} for c in cores],
        core_ids=cores).results
    y_all = [np.ascontiguousarray(np.concatenate(
        [np.asarray(r3[c]["y_out"])[b * 256:(b + 1) * 256] for c in cores], axis=0))
        for b in range(2)]

    nc = bass.Bass("TRN2", target_bir_lowering=False, debug=False)
    build_L4(nc)
    in_maps = []
    for c in cores:
        bc, s0 = c // 4, (c % 4) * TOK
        in_maps.append({
            "y_cols": np.ascontiguousarray(y_all[bc][:, s0:s0 + TOK]),
            "x_fm": P["x_fm"][c], "outwT": P["outwT"], "gamma": P["gamma"],
            "beta": P["beta"], "gate_w": P["gate_w"], "w1": P["w1"], "b1c": P["b1c"],
            "w2": P["w2"], "b2row": P["b2row"], "wattnT": P["wattnT"], "ident": ident})
    r4 = run_bass_kernel_spmd(nc, in_maps, core_ids=cores).results
    qf, kf, vf = [], [], []
    for b in range(2):
        qs = np.concatenate([np.asarray(r4[4 * b + j]["qk_out"])[0:1024]
                             for j in range(4)], axis=1)
        ks = np.concatenate([np.asarray(r4[4 * b + j]["qk_out"])[1024:2048]
                             for j in range(4)], axis=1)
        vs = np.concatenate([np.asarray(r4[4 * b + j]["v_out"]) for j in range(4)], axis=0)
        qf.append(qs); kf.append(ks); vf.append(vs)

    nc = bass.Bass("TRN2", target_bir_lowering=False, debug=False)
    build_L5(nc)
    in_maps = []
    for c in cores:
        gs = [4 * c + p for p in range(4)]
        qp = np.stack([qf[g // 16][64 * (g % 16):64 * (g % 16) + 64] for g in gs])
        kp = np.stack([kf[g // 16][64 * (g % 16):64 * (g % 16) + 64] for g in gs])
        vp = np.stack([vf[g // 16][:, 64 * (g % 16):64 * (g % 16) + 64] for g in gs])
        in_maps.append({"q_p": np.ascontiguousarray(qp), "k_p": np.ascontiguousarray(kp),
                        "v_p": np.ascontiguousarray(vp), "dmask": P["dmask"]})
    r5 = run_bass_kernel_spmd(nc, in_maps, core_ids=cores).results
    ya = [np.ascontiguousarray(np.concatenate(
        [np.asarray(r5[4 * b + j]["y_p"]) for j in range(4)], axis=0)) for b in range(2)]

    nc = bass.Bass("TRN2", target_bir_lowering=False, debug=False)
    build_L6(nc)
    in_maps = []
    for c in cores:
        bc, s0 = c // 4, (c % 4) * TOK
        in_maps.append({
            "yat_cols": np.ascontiguousarray(ya[bc][:, s0:s0 + TOK]),
            "x2_fm": np.asarray(r4[c]["x2_out"]), "wprojT": P["wprojT"],
            "W_base": P["W_base"], "bbase": P["bbase"], "Wd1": P["Wd1"],
            "bd1c": P["bd1c"], "Wd2": P["Wd2"], "bd2c": P["bd2c"]})
    r6 = run_bass_kernel_spmd(nc, in_maps, core_ids=cores).results
    lparts = np.ascontiguousarray(np.stack(
        [np.asarray(r6[c]["loss_out"])[0] for c in cores]))

    nc = bass.Bass("TRN2", target_bir_lowering=False, debug=False)
    build_L7(nc)
    in_maps = []
    for c in cores:
        in_maps.append({
            "lossparts": lparts, "selb": P["selb"],
            "x3b": np.asarray(r6[c]["x3b_out"]), "bx3": np.asarray(r6[c]["bx3_out"]),
            "Wad": P["Wad"], "badc": P["badc"], "Wau": P["Wau"], "baup8": P["baup8"]})
    r7 = run_bass_kernel_spmd(nc, in_maps, core_ids=cores).results

    out = np.empty((B * S, D), np.float32)
    for c in cores:
        out[c * TOK:(c + 1) * TOK] = np.asarray(r7[c]["out_slab"], np.float32).T
    return out.reshape(B, S, D)
